# version 01
# baseline (speedup 1.0000x reference)
import contextlib
import numpy as np
import ml_dtypes
import concourse.bass as bass
import concourse.mybir as mybir
from concourse.bass_utils import run_bass_kernel_spmd

F32 = mybir.dt.float32
BF16 = mybir.dt.bfloat16
I32 = mybir.dt.int32
AF = mybir.ActivationFunctionType
ALU = mybir.AluOpType
AX = mybir.AxisListType
BF = ml_dtypes.bfloat16

D = 4096
M = 2048
KT = 32
EPS = 1e-5
ENGINES = ("sync", "gpsimd", "tensor", "vector", "scalar")


class Plan:
    def __init__(self, nc, stack):
        self.nc = nc
        self.q = {e: [] for e in ENGINES}
        self.s = {}
        self.c = {}
        self.stack = stack

    def sem(self, *names):
        for n in names:
            if n in self.s:
                continue
            self.s[n] = self.stack.enter_context(self.nc.semaphore(n))
            self.c[n] = 0

    def emit(self, eng, fn):
        self.q[eng].append(fn)

    def wait(self, eng, name, val):
        if val <= 0:
            return
        if name.startswith("d"):
            val = self.c[name]
        h = self.s[name]
        self.emit(eng, lambda e, h=h, v=val: e.wait_ge(h, v))

    def op(self, eng, fn, inc=None, n=1):
        if inc is None:
            self.emit(eng, lambda e, fn=fn: fn(e))
            return self.c.get(inc, 0)
        self.c[inc] += n
        h = self.s[inc]
        self.emit(eng, lambda e, fn=fn, h=h, n=n: fn(e).then_inc(h, n))
        return self.c[inc]

    ESEM = {"vector": "ve", "scalar": "se", "tensor": "pe"}

    def seq(self, eng, fn):
        sem = self.ESEM[eng]
        self.wait(eng, sem, self.c[sem])
        return self.op(eng, fn, inc=sem)

    def dma(self, eng, out, in_, inc, **kw):
        self.sem(inc)
        return self.op(eng, lambda e, o=out, i=in_, kw=kw: e.dma_start(out=o, in_=i, **kw),
                       inc=inc, n=16)

    def dma_done_waits(self, eng):
        for n in list(self.s):
            if n.startswith("d") and self.c[n] > 0:
                self.wait(eng, n, self.c[n])

    def finalize(self, block):
        for eng in ENGINES:
            qq = self.q[eng]
            if not qq:
                continue
            def mk(qq):
                def _(e):
                    for f in qq:
                        f(e)
                return _
            getattr(block, eng)(mk(qq))


def stats_finalize(p, sum_ps, sq_ps, murow, invrow, trow1, trow2, nfeat, W):
    p.seq("vector", lambda v: v.tensor_scalar_mul(murow[0:1, 0:W], sum_ps, 1.0 / nfeat))
    ve_ps = p.seq("vector", lambda v: v.tensor_scalar_mul(trow1[0:1, 0:W], sq_ps, 1.0 / nfeat))
    p.seq("vector", lambda v: v.tensor_tensor(trow2[0:1, 0:W], murow[0:1, 0:W], murow[0:1, 0:W],
                                              ALU.mult))
    p.seq("vector", lambda v: v.tensor_tensor(trow1[0:1, 0:W], trow1[0:1, 0:W],
                                              trow2[0:1, 0:W], ALU.subtract))
    ve = p.seq("vector", lambda v: v.tensor_scalar(trow1[0:1, 0:W], trow1[0:1, 0:W], EPS, None,
                                                   ALU.add))
    p.wait("scalar", "ve", ve)
    se1 = p.op("scalar", lambda s: s.activation(trow2[0:1, 0:W], trow1[0:1, 0:W], AF.Ln),
               inc="se")
    p.wait("vector", "se", se1)
    ve2 = p.op("vector", lambda v: v.tensor_scalar(trow1[0:1, 0:W], trow2[0:1, 0:W], -0.5, None,
                                                   ALU.mult), inc="ve")
    p.wait("scalar", "ve", ve2)
    se = p.op("scalar", lambda s: s.activation(invrow[0:1, 0:W], trow1[0:1, 0:W], AF.Exp),
              inc="se")
    return ve_ps, se


def bcast_rows(p, ps, ones1, rows_planes, extra_wait_se=0):
    p.wait("tensor", "ve", p.c["ve"])
    if extra_wait_se:
        p.wait("tensor", "se", extra_wait_se)
    ve = 0
    for row, plane in rows_planes:
        W = plane.shape[1]
        for cidx in range((W + 511) // 512):
            c0, c1 = cidx * 512, min((cidx + 1) * 512, W)
            pe = p.op("tensor", lambda t, r=row, c0=c0, c1=c1: t.matmul(
                ps[:, c0:c1], ones1[:], r[0:1, c0:c1], start=True, stop=True), inc="pe")
            p.wait("vector", "pe", pe)
            ve = p.op("vector", lambda v, pl=plane, c0=c0, c1=c1: v.tensor_copy(
                pl[:, c0:c1], ps[:, c0:c1]), inc="ve")
    return ve



def build_ffn_launch(kind):
    nc = bass.Bass("TRN2", target_bir_lowering=False)
    st = contextlib.ExitStack()
    with st:
        if kind == "A":
            vin = nc.dram_tensor("xT", [D, M], F32, kind="ExternalInput").ap()
            wT = nc.dram_tensor("wT", [D, D], F32, kind="ExternalInput").ap()
            maskT = nc.dram_tensor("maskT", [D, D], I32, kind="ExternalInput").ap()
            sel4_d = nc.dram_tensor("sel4", [128, 4], F32, kind="ExternalInput").ap()
            sel4t_d = nc.dram_tensor("sel4t", [4, 128], BF16, kind="ExternalInput").ap()
            r32f_d = nc.dram_tensor("r32f", [128, D], BF16, kind="ExternalInput").ap()
            ident_d = nc.dram_tensor("ident", [4, 4], F32, kind="ExternalInput").ap()
            wmask = nc.dram_tensor("wmask", [D, D], BF16, kind="Internal").ap()
        else:
            vin = nc.dram_tensor("vin", [D, M], BF16, kind="ExternalInput").ap()
            wT = nc.dram_tensor("wT", [D, D], F32, kind="ExternalInput").ap()
            bias_d = nc.dram_tensor("biasb", [128, 32], F32, kind="ExternalInput").ap()
        ER = {"B": 48, "D": 128}.get(kind, 0)
        if kind == "B":
            acat_d = nc.dram_tensor("acatT", [D, ER], F32, kind="ExternalInput").ap()
            bcat_d = nc.dram_tensor("bcatL", [ER, D], F32, kind="ExternalInput").ap()
            lsi_d = nc.dram_tensor("lsi", [1, 4], I32, kind="ExternalInput").ap()
            ccst_d = nc.dram_tensor("ccst", [1, 8], F32, kind="ExternalInput").ap()
            ero3_d = nc.dram_tensor("ero3", [1, 3], F32, kind="ExternalInput").ap()
            sel16_d = nc.dram_tensor("sel16", [3, ER], F32, kind="ExternalInput").ap()
            srow_scr = nc.dram_tensor("srow_scr", [1, 3], F32, kind="Internal").ap()
        if kind == "D":
            acat_d = nc.dram_tensor("acatT", [D, ER], F32, kind="ExternalInput").ap()
            bcat_d = nc.dram_tensor("bcatL", [ER, D], F32, kind="ExternalInput").ap()
            lpart_d = nc.dram_tensor("lpart", [1, 8], F32, kind="ExternalInput").ap()
            rb_d = nc.dram_tensor("rb", [1, 8], F32, kind="ExternalInput").ap()
            sel16t_d = nc.dram_tensor("sel16t", [8, 128], F32, kind="ExternalInput").ap()
            mu_in = nc.dram_tensor("muin", [1, M], F32, kind="ExternalInput").ap()
            inv_in = nc.dram_tensor("invin", [1, M], F32, kind="ExternalInput").ap()
            scr8 = nc.dram_tensor("scr8", [1, 8], F32, kind="Internal").ap()
            v4scr = nc.dram_tensor("v4scr", [D, M], BF16, kind="Internal").ap()
            xout = nc.dram_tensor("xout", [D, M], F32, kind="ExternalOutput").ap()
        else:
            vout = nc.dram_tensor("vout", [D, M], BF16, kind="ExternalOutput").ap()

        slab = st.enter_context(nc.sbuf_tensor("slab", [128, KT * M], BF16))
        if kind != "A":
            murow = st.enter_context(nc.sbuf_tensor("murow", [1, M], F32))
            invrow = st.enter_context(nc.sbuf_tensor("invrow", [1, M], F32))
        onesb = st.enter_context(nc.sbuf_tensor("onesb", [128, 1], BF16))
        ones1 = st.enter_context(nc.sbuf_tensor("ones1", [1, 128], F32))
        PM = st.enter_context(nc.psum_tensor("PM", [128, 2048], F32))
        PM2 = st.enter_context(nc.psum_tensor("PM2", [128, 2048], F32))

        p = Plan(nc, st)
        p.sem("dg", "ds", "pe", "ve", "se")
        blk_cm = nc.Block()

        p.op("vector", lambda v: v.memset(onesb[:], 1.0), inc="ve")
        p.op("vector", lambda v: v.memset(ones1[:], 1.0), inc="ve")
        ve_const = p.c["ve"]

        slab3 = slab[:].rearrange("p (kt m) -> p kt m", kt=KT)
        vin3 = vin.rearrange("(kt p) m -> p kt m", p=128)

        dg_slab = []
        for i in range(8):
            dgv = p.dma("gpsimd", slab3[:, 4 * i:4 * i + 4, :], vin3[:, 4 * i:4 * i + 4, :],
                        inc="dsl")
            dg_slab.append(dgv)

        wblk = [st.enter_context(nc.sbuf_tensor(f"wblk{i}", [128, KT * 128], BF16))
                for i in range(2)]
        vbf = [st.enter_context(nc.sbuf_tensor(f"vbf{i}", [128, M], BF16)) for i in range(2)]
        if kind != "A":
            trow1 = wblk[0][:].bitcast(F32)[0:1, :]
            trow2 = wblk[1][:].bitcast(F32)[0:1, :]

        if kind == "A":
            scr1 = st.enter_context(nc.sbuf_tensor("scr1", [128, 8192], BF16))
            scr2 = st.enter_context(nc.sbuf_tensor("scr2", [128, 8192], BF16))
            mrow = [scr1[:, 0:4096].bitcast(F32), scr1[:, 4096:8192].bitcast(F32)]
            wdma = [scr2[:, 0:4096].bitcast(F32), scr2[:, 4096:8192].bitcast(F32)]
            wm = [wblk[0][:, 0:2048], wblk[0][:, 2048:4096]]
            binfk = st.enter_context(nc.sbuf_tensor("binfk", [4, D], BF16))
            mred = st.enter_context(nc.sbuf_tensor("mred", [128, 128], F32))
            stg = st.enter_context(nc.sbuf_tensor("stg", [4, 128], F32))
            bmt = st.enter_context(nc.sbuf_tensor("bmt", [128, 128], BF16))
            sel4 = st.enter_context(nc.sbuf_tensor("sel4_s", [128, 4], F32))
            sel4t = st.enter_context(nc.sbuf_tensor("sel4t_s", [4, 128], BF16))
            r32f = st.enter_context(nc.sbuf_tensor("r32f_s", [128, D], BF16))
            ident = st.enter_context(nc.sbuf_tensor("ident_s", [4, 4], F32))

            p.dma("sync", sel4[:], sel4_d, inc="dcon")
            p.dma("sync", sel4t[:], sel4t_d, inc="dcon")
            p.dma("sync", r32f[:], r32f_d, inc="dcon")
            ds_consts = p.dma("sync", ident[:], ident_d, inc="dcon")

            ve_red = [[0, 0] for _ in range(KT)]
            pe_sel = [0] * KT
            ve_gt = [0] * KT
            pe_tr = [0] * KT
            ve_bmtc = [0] * KT
            for i in range(KT):
                for h in range(2):
                    if i >= 1:
                        p.wait("gpsimd", "ve", ve_red[i - 1][h])
                    dg = p.dma("gpsimd", mrow[h][:],
                               maskT[i * 128:(i + 1) * 128, h * 2048:(h + 1) * 2048],
                               inc=f"dm{h}")
                    p.wait("vector", f"dm{h}", dg)
                    if i >= 1:
                        p.wait("vector", "pe", pe_sel[i - 1])
                    ve_red[i][h] = p.op("vector", lambda v, i=i, h=h: v.tensor_reduce(
                        mred[:, h * 64:(h + 1) * 64],
                        mrow[h][:].rearrange("p (nb w) -> p nb w", w=32), AX.X, ALU.add),
                        inc="ve")
                p.wait("tensor", "ve", ve_red[i][1])
                if i == 0:
                    p.wait("tensor", "dcon", ds_consts)
                if i >= 1:
                    p.wait("tensor", "ve", ve_gt[i - 1])
                pe_sel[i] = p.op("tensor", lambda t: t.matmul(
                    PM[0:4, 0:128], sel4[:], mred[:], start=True, stop=True), inc="pe")
                p.wait("vector", "pe", pe_sel[i])
                ve_gt[i] = p.op("vector", lambda v: v.tensor_scalar(
                    stg[:], PM[0:4, 0:128], 0.0, None, ALU.is_gt), inc="ve")
                p.wait("tensor", "ve", ve_gt[i])
                if i >= 1:
                    p.wait("tensor", "ve", ve_bmtc[i - 1])
                pe_tr[i] = p.op("tensor", lambda t: t.transpose(
                    PM[:, 512:516], stg[:], ident[:]), inc="pe")
                p.wait("vector", "pe", pe_tr[i])
                ve_bmtc[i] = p.op("vector", lambda v, i=i: v.tensor_copy(
                    bmt[:, 4 * i:4 * i + 4], PM[:, 512:516]), inc="ve")
            ve_bmt = p.c["ve"]

            se_copy = {}
            pe_em = {}
            ve_mult = {}
            ds_store = {}
            last_mult_reg = {0: 0, 1: 0}
            p.wait("tensor", "ve", ve_bmt)
            for kt in range(KT):
                for c in range(8):
                    reg0 = 512 + (c % 2) * 512
                    p.wait("tensor", "se", se_copy.get((kt, c - 2), 0))
                    if c == 0 and kt >= 1:
                        p.wait("tensor", "pe", pe_em.get((kt - 1, 3), 0))
                    pe = p.op("tensor", lambda t, kt=kt, r=reg0, c=c: t.matmul(
                        PM[0:4, r:r + 512], bmt[:, 4 * kt:4 * kt + 4],
                        r32f[:, c * 512:(c + 1) * 512], start=True, stop=True), inc="pe")
                    p.wait("scalar", "pe", pe)
                    if c == 0 and kt >= 1:
                        p.wait("scalar", "pe", pe_em[(kt - 1, 3)])
                    se_copy[(kt, c)] = p.op("scalar", lambda s, c=c, r=reg0: s.copy(
                        binfk[0:4, c * 512:(c + 1) * 512], PM[0:4, r:r + 512]), inc="se")
                for h in range(2):
                    if kt >= 1:
                        p.wait("sync", "ve", ve_mult.get((kt - 1, 2 * h + 1), 0))
                    dsw = p.dma("sync", wdma[h],
                                wT[kt * 128:(kt + 1) * 128, h * 2048:(h + 1) * 2048],
                                inc=f"dw{h}")
                    for qq in range(2):
                        q = 2 * h + qq
                        reg = (q % 2) * 1024
                        p.wait("tensor", "se", se_copy[(kt, 2 * q + 1)])
                        p.wait("tensor", "ve", last_mult_reg[q % 2])
                        for c2 in range(2):
                            pe_em[(kt, q)] = p.op("tensor", lambda t, kt=kt, q=q, c2=c2, reg=reg:
                                t.matmul(PM2[:, reg + c2 * 512: reg + (c2 + 1) * 512],
                                         sel4t[:],
                                         binfk[0:4, q * 1024 + c2 * 512:
                                               q * 1024 + (c2 + 1) * 512],
                                         start=True, stop=True), inc="pe")
                        p.wait("vector", "pe", pe_em[(kt, q)])
                        p.wait("vector", f"dw{h}", dsw)
                        p.wait("vector", f"dso{h}", ds_store.get((kt - 1, h), 0))
                        ve_mult[(kt, q)] = p.op("vector", lambda v, h=h, qq=qq, reg=reg:
                            v.tensor_tensor(wm[h][:, qq * 1024:(qq + 1) * 1024],
                                            wdma[h][:, qq * 1024:(qq + 1) * 1024],
                                            PM2[:, reg:reg + 1024], ALU.mult), inc="ve")
                        last_mult_reg[q % 2] = ve_mult[(kt, q)]
                    p.wait("sync", "ve", ve_mult[(kt, 2 * h + 1)])
                    ds_store[(kt, h)] = p.dma("sync",
                                              wmask[kt * 128:(kt + 1) * 128,
                                                    h * 2048:(h + 1) * 2048],
                                              wm[h], inc=f"dso{h}")
            ds_prepass = 0
            wsrc = wmask
        else:
            wsrc = wT
            ds_prepass = 0

        if kind != "A":
            mub = st.enter_context(nc.sbuf_tensor("mub", [128, M], BF16))
            invb = st.enter_context(nc.sbuf_tensor("invb", [128, M], BF16))
        if kind != "A":
            sqb = [st.enter_context(nc.sbuf_tensor("sqb0", [128, M], BF16))] * 2
            biasb = st.enter_context(nc.sbuf_tensor("biasb_s", [128, 32], F32))
            acat = st.enter_context(nc.sbuf_tensor("acat", [128, KT * ER], BF16))
            bcat = st.enter_context(nc.sbuf_tensor("bcat", [ER, D], BF16))
            tbf = st.enter_context(nc.sbuf_tensor("tbf", [ER, M], BF16))
            scol = st.enter_context(nc.sbuf_tensor("scol", [ER, 1], F32))
            small = st.enter_context(nc.sbuf_tensor("small", [1, 64], F32))
            s31 = st.enter_context(nc.sbuf_tensor("s31", [8, 2], F32))
            selw = st.enter_context(nc.sbuf_tensor("selw", [8, 128], F32))
        if kind == "D":
            xo = wblk[0][:].bitcast(F32)

        if kind == "B":
            pe_stk = [0] * KT
            for kt in range(KT):
                p.wait("scalar", "dsl", dg_slab[kt // 4])
                if kt >= 1:
                    p.wait("scalar", "pe", pe_stk[kt - 1])
                se = p.op("scalar", lambda s, kt=kt: s.activation(
                    sqb[kt % 2][:], slab[:, kt * M:(kt + 1) * M], AF.Square), inc="se")
                p.wait("tensor", "se", se)
                if kt == 0:
                    p.wait("tensor", "ve", ve_const)
                    p.wait("tensor", "dsl", dg_slab[0])
                for mc in range(4):
                    p.op("tensor", lambda t, kt=kt, mc=mc: t.matmul(
                        PM[0:1, mc * 512:(mc + 1) * 512], onesb[:],
                        slab[:, kt * M + mc * 512: kt * M + (mc + 1) * 512],
                        start=(kt == 0), stop=(kt == KT - 1)), inc="pe")
                    pe_stk[kt] = p.op("tensor", lambda t, kt=kt, mc=mc: t.matmul(
                        PM2[0:1, mc * 512:(mc + 1) * 512], onesb[:],
                        sqb[kt % 2][:, mc * 512:(mc + 1) * 512],
                        start=(kt == 0), stop=(kt == KT - 1)), inc="pe")
            p.wait("vector", "pe", p.c["pe"])
            _, se_fin = stats_finalize(p, PM[0:1, 0:M], PM2[0:1, 0:M], murow, invrow,
                                       trow1, trow2, D, M)
        if kind == "D":
            p.dma("sync", murow[:], mu_in, inc="dcon")
            ds_mu = p.dma("sync", invrow[:], inv_in, inc="dcon")
            p.wait("tensor", "dcon", ds_mu)
            se_fin = 0

        if kind != "A":
            bcast_rows(p, PM2, ones1, [(murow, mub), (invrow, invb)], extra_wait_se=se_fin)
            ve_norm = [0] * KT
            for kt in range(KT):
                p.wait("vector", "dsl", dg_slab[kt // 4])
                p.op("vector", lambda v, kt=kt: v.tensor_tensor(
                    slab[:, kt * M:(kt + 1) * M], slab[:, kt * M:(kt + 1) * M], mub[:],
                    ALU.subtract), inc="ve")
                ve_norm[kt] = p.seq("vector", lambda v, kt=kt: v.tensor_tensor(
                    slab[:, kt * M:(kt + 1) * M], slab[:, kt * M:(kt + 1) * M], invb[:],
                    ALU.mult))
            ve_slab_ready = p.c["ve"]
        else:
            ve_slab_ready = ve_const

        ve_tbf = 0
        ds_biasb = 0
        if kind != "A":
            ds_biasb = p.dma("sync", biasb[:], bias_d, inc="dcon")
            acat3 = acat[:].rearrange("p (kt r) -> p kt r", kt=KT)
            acatd3 = acat_d.rearrange("(kt p) r -> p kt r", p=128)
            for i in range(4):
                p.dma("gpsimd", acat3[:, 8 * i:8 * i + 8, :], acatd3[:, 8 * i:8 * i + 8, :],
                      inc="dac")
            dg_acat = p.c["dac"]
            p.dma("gpsimd", bcat[:], bcat_d, inc="dac")
            dg_bcat = p.c["dac"]

            if kind == "B":
                p.dma("gpsimd", small[0:1, 0:4].bitcast(I32), lsi_d, inc="dac")
                dg_lsi = p.c["dac"]
                p.dma("sync", small[0:1, 8:16], ccst_d, inc="dcon")
                p.dma("sync", small[0:1, 16:19], ero3_d, inc="dcon")
                ds_sel = p.dma("sync", selw[0:3, 0:ER], sel16_d, inc="dcon")
                p.wait("vector", "dac", dg_lsi)
                p.wait("vector", "dcon", ds_sel)
                sm = small
                p.seq("vector", lambda v: v.tensor_copy(sm[0:1, 4:8], sm[0:1, 0:4].bitcast(I32)))
                p.seq("vector", lambda v: v.tensor_scalar(sm[0:1, 20:24], sm[0:1, 4:8],
                                                         sm[0:1, 8:9], None, ALU.is_lt))
                c0, c1, c2, c3 = (sm[0:1, 20 + j:21 + j] for j in range(4))
                p.seq("vector", lambda v: v.tensor_scalar(sm[0:1, 24:25], c3, 3.0, None, ALU.add))
                p.seq("vector", lambda v: v.tensor_tensor(sm[0:1, 25:26], c2, sm[0:1, 24:25],
                                                         ALU.mult))
                p.seq("vector", lambda v: v.tensor_scalar(sm[0:1, 26:27], c1, 2.0, None,
                                                         ALU.mult))
                p.seq("vector", lambda v: v.tensor_scalar(sm[0:1, 27:28], c1, -1.0, 1.0, ALU.mult,
                                                         ALU.add))
                p.seq("vector", lambda v: v.tensor_tensor(sm[0:1, 28:29], sm[0:1, 27:28], c0,
                                                         ALU.mult))
                p.seq("vector", lambda v: v.tensor_tensor(sm[0:1, 29:30], sm[0:1, 26:27],
                                                         sm[0:1, 28:29], ALU.add))
                p.seq("vector", lambda v: v.tensor_scalar(sm[0:1, 30:31], c2, -1.0, 1.0, ALU.mult,
                                                         ALU.add))
                p.seq("vector", lambda v: v.tensor_tensor(sm[0:1, 31:32], sm[0:1, 30:31],
                                                         sm[0:1, 29:30], ALU.mult))
                p.seq("vector", lambda v: v.tensor_tensor(sm[0:1, 32:33], sm[0:1, 25:26],
                                                         sm[0:1, 31:32], ALU.add))
                r_ap = sm[0:1, 32:33]
                p.seq("vector", lambda v: v.tensor_scalar(sm[0:1, 33:34], r_ap, 1.5, None,
                                                         ALU.is_gt))
                p.seq("vector", lambda v: v.tensor_scalar(sm[0:1, 34:35], r_ap, -2.0, 0.0,
                                                         ALU.add, ALU.max))
                p.seq("vector", lambda v: v.tensor_scalar(sm[0:1, 35:36], sm[0:1, 34:35], 2.0,
                                                         None, ALU.min))
                ve1 = p.seq("vector", lambda v: v.tensor_scalar(
                    sm[0:1, 36:39], sm[0:1, 16:19], sm[0:1, 35:36], None, ALU.subtract))
                p.wait("scalar", "ve", ve1)
                se_abs = p.op("scalar", lambda s: s.activation(sm[0:1, 40:43], sm[0:1, 36:39],
                                                               AF.Abs), inc="se")
                p.wait("vector", "se", se_abs)
                p.seq("vector", lambda v: v.tensor_scalar(sm[0:1, 44:47], sm[0:1, 40:43], -1.0,
                                                         1.0, ALU.mult, ALU.add))
                p.seq("vector", lambda v: v.tensor_scalar(sm[0:1, 48:51], sm[0:1, 44:47], 0.0,
                                                         None, ALU.max))
                ve_srow = p.seq("vector", lambda v: v.tensor_scalar(
                    sm[0:1, 52:55], sm[0:1, 48:51], sm[0:1, 33:34], None, ALU.mult))
                p.wait("sync", "ve", ve_srow)
                p.dma("sync", srow_scr, sm[0:1, 52:55], inc="dsr")
                p.wait("sync", "dsr", p.c["dsr"])
                ds_s3 = p.dma("sync", s31[0:3, 0:1], srow_scr.rearrange("one e -> e one"),
                              inc="dsr", allow_slow_non_contiguous=True)
                p.wait("tensor", "dsr", ds_s3)
                pe = p.op("tensor", lambda t: t.matmul(PM2[0:ER, 0:1], selw[0:3, 0:ER],
                                                       s31[0:3, 0:1], start=True, stop=True),
                          inc="pe")
                p.wait("vector", "pe", pe)
                ve_scol = p.op("vector", lambda v: v.tensor_copy(scol[:], PM2[0:ER, 0:1]),
                               inc="ve")
            else:
                p.dma("sync", small[0:1, 0:8], lpart_d, inc="dcon")
                p.dma("sync", small[0:1, 8:16], rb_d, inc="dcon")
                ds_l = p.dma("sync", selw[:], sel16t_d, inc="dcon")
                p.wait("vector", "dcon", ds_l)
                sm = small
                p.seq("vector", lambda v: v.tensor_tensor(sm[0:1, 16:24], sm[0:1, 0:8],
                                                          sm[0:1, 8:16], ALU.add))
                p.seq("vector", lambda v: v.tensor_reduce(sm[0:1, 24:25], sm[0:1, 16:24], AX.X,
                                                          ALU.max))
                ve1 = p.seq("vector", lambda v: v.tensor_scalar(
                    sm[0:1, 32:40], sm[0:1, 16:24], sm[0:1, 24:25], None, ALU.subtract))
                p.wait("scalar", "ve", ve1)
                se1 = p.op("scalar", lambda s: s.activation(sm[0:1, 40:48], sm[0:1, 32:40],
                                                            AF.Exp), inc="se")
                p.wait("vector", "se", se1)
                p.seq("vector", lambda v: v.tensor_reduce(sm[0:1, 48:49], sm[0:1, 40:48], AX.X,
                                                          ALU.add))
                p.seq("vector", lambda v: v.reciprocal(sm[0:1, 49:50], sm[0:1, 48:49]))
                ve2 = p.seq("vector", lambda v: v.tensor_scalar(
                    sm[0:1, 56:64], sm[0:1, 40:48], sm[0:1, 49:50], None, ALU.mult))
                p.wait("sync", "ve", ve2)
                p.dma("sync", scr8, sm[0:1, 56:64], inc="dsr")
                p.wait("sync", "dsr", p.c["dsr"])
                ds2 = p.dma("sync", s31[0:8, 0:1], scr8.rearrange("one e -> e one"),
                            inc="dsr", allow_slow_non_contiguous=True)
                p.wait("tensor", "dsr", ds2)
                pe = p.op("tensor", lambda t: t.matmul(PM2[0:128, 0:1], selw[:], s31[0:8, 0:1],
                                                       start=True, stop=True), inc="pe")
                p.wait("vector", "pe", pe)
                ve_scol = p.op("vector", lambda v: v.tensor_scalar_mul(scol[:], PM2[0:128, 0:1],
                                                                       2.0), inc="ve")

            p.wait("tensor", "ve", ve_slab_ready)
            p.wait("tensor", "dg", dg_acat)
            for kt in range(KT):
                for mc in range(4):
                    p.op("tensor", lambda t, kt=kt, mc=mc: t.matmul(
                        PM[0:ER, mc * 512:(mc + 1) * 512],
                        acat[:, kt * ER:(kt + 1) * ER],
                        slab[:, kt * M + mc * 512: kt * M + (mc + 1) * 512],
                        start=(kt == 0), stop=(kt == KT - 1)), inc="pe")
            pe_t = p.c["pe"]
            p.wait("vector", "pe", pe_t)
            p.wait("vector", "ve", ve_scol)
            ve_tbf = p.op("vector", lambda v: v.tensor_scalar(
                tbf[:], PM[0:ER, 0:M], scol[:], None, ALU.mult), inc="ve")

        wsrc3 = wsrc.rearrange("(kt p) n -> p kt n", p=128)
        wblk3 = [w[:].rearrange("p (kt n) -> p kt n", kt=KT) for w in wblk]
        w_eng = "gpsimd" if kind != "A" else "sync"
        pass
        ds_wb = [0] * KT
        pe_nt = [0] * KT
        ve_evac = [0] * KT
        ds_vout = [0] * KT
        for t in range(KT):
            if t >= 2:
                p.wait(w_eng, "pe", pe_nt[t - 2])
            if t < 2 and kind != "A":
                p.wait(w_eng, "ve", ve_slab_ready)
            for i in range(4):
                ds_wb[t] = p.dma(w_eng, wblk3[t % 2][:, 8 * i:8 * i + 8, :],
                                 wsrc3[:, 8 * i:8 * i + 8, t * 128:(t + 1) * 128],
                                 inc=f"dwa{t % 2}")
            if t < 2 and kind != "A":
                pass
            ps = PM if t % 2 == 0 else PM2
            p.wait("tensor", f"dwa{t % 2}", ds_wb[t])
            if t == 0:
                p.wait("tensor", "ve", max(ve_slab_ready, ve_tbf))
                if kind == "A":
                    p.wait("tensor", "dsl", dg_slab[-1])
            if t >= 2:
                p.wait("tensor", "ve", ve_evac[t - 2])
            for kt in range(KT):
                for mc in range(4):
                    last = (kt == KT - 1) and kind == "A"
                    p.op("tensor", lambda t_, t=t, kt=kt, mc=mc, ps=ps, last=last: t_.matmul(
                        ps[:, mc * 512:(mc + 1) * 512],
                        wblk[t % 2][:, kt * 128:(kt + 1) * 128],
                        slab[:, kt * M + mc * 512: kt * M + (mc + 1) * 512],
                        start=(kt == 0), stop=last),
                        inc="pe" if (last and mc == 3) else None)
            if kind != "A":
                for mc in range(4):
                    p.op("tensor", lambda t_, t=t, mc=mc, ps=ps: t_.matmul(
                        ps[:, mc * 512:(mc + 1) * 512], bcat[:, t * 128:(t + 1) * 128],
                        tbf[:, mc * 512:(mc + 1) * 512], start=False, stop=True),
                        inc="pe" if mc == 3 else None)
            pe_nt[t] = p.c["pe"]
            p.wait("vector", "pe", pe_nt[t])
            if t == 0 and kind != "A":
                p.wait("vector", "dcon", ds_biasb)
            if t >= 2:
                p.wait("vector", f"dva{t % 2}", ds_vout[t - 2])
            if kind == "A":
                p.op("vector", lambda v, t=t, ps=ps: v.tensor_tensor(
                    vbf[t % 2][:], ps[0:128, 0:M], slab[:, t * M:(t + 1) * M], ALU.add),
                    inc="ve")
            else:
                p.op("vector", lambda v, t=t, ps=ps: v.scalar_tensor_tensor(
                    vbf[t % 2][:], ps[0:128, 0:M], biasb[:, t:t + 1],
                    slab[:, t * M:(t + 1) * M], ALU.add, ALU.add), inc="ve")
            ve_evac[t] = p.c["ve"]
            p.wait("sync", "ve", ve_evac[t])
            tgt = v4scr if kind == "D" else vout
            ds_vout[t] = p.dma("sync", tgt[t * 128:(t + 1) * 128, :], vbf[t % 2][:],
                               inc=f"dva{t % 2}")

        if kind == "D":
            pe_st = [0] * KT
            for kt in range(KT):
                p.wait("gpsimd", f"dva{kt % 2}", ds_vout[kt])
                if kt >= 2:
                    p.wait("gpsimd", "pe", pe_st[kt - 2])
                dgv = p.dma("gpsimd", vbf[kt % 2][:], v4scr[kt * 128:(kt + 1) * 128, :],
                            inc=f"dre{kt % 2}")
                p.wait("scalar", f"dre{kt % 2}", dgv)
                if kt >= 1:
                    p.wait("scalar", "pe", pe_st[kt - 1])
                se = p.op("scalar", lambda s, kt=kt: s.activation(
                    sqb[kt % 2][:], vbf[kt % 2][:], AF.Square), inc="se")
                p.wait("tensor", "se", se)
                if kt == 0:
                    p.wait("tensor", "ve", ve_evac[KT - 1])
                for mc in range(4):
                    p.op("tensor", lambda t, kt=kt, mc=mc: t.matmul(
                        PM[0:1, mc * 512:(mc + 1) * 512], onesb[:],
                        vbf[kt % 2][:, mc * 512:(mc + 1) * 512],
                        start=(kt == 0), stop=(kt == KT - 1)), inc="pe")
                    pe_st[kt] = p.op("tensor", lambda t, kt=kt, mc=mc: t.matmul(
                        PM2[0:1, mc * 512:(mc + 1) * 512], onesb[:],
                        sqb[kt % 2][:, mc * 512:(mc + 1) * 512],
                        start=(kt == 0), stop=(kt == KT - 1)), inc="pe")
            p.wait("vector", "pe", p.c["pe"])
            _, se_fin4 = stats_finalize(p, PM[0:1, 0:M], PM2[0:1, 0:M], murow, invrow,
                                        trow1, trow2, D, M)
            bcast_rows(p, PM2, ones1, [(murow, mub), (invrow, invb)], extra_wait_se=se_fin4)
            ds_fin = [0] * KT
            ve_swp = [0] * KT
            pe_stats_done = p.c["pe"]
            for kt in range(KT):
                p.wait("gpsimd", "pe", pe_stats_done)
                if kt >= 2:
                    p.wait("gpsimd", "ve", ve_swp[kt - 2])
                dgv = p.dma("gpsimd", vbf[kt % 2][:], v4scr[kt * 128:(kt + 1) * 128, :],
                            inc=f"dre{kt % 2}")
                p.wait("vector", f"dre{kt % 2}", dgv)
                if kt >= 1:
                    p.wait("vector", "dxo", ds_fin[kt - 1])
                p.op("vector", lambda v, kt=kt: v.tensor_tensor(
                    vbf[kt % 2][:], vbf[kt % 2][:], mub[:], ALU.subtract), inc="ve")
                ve = p.seq("vector", lambda v, kt=kt: v.tensor_tensor(
                    xo[:], vbf[kt % 2][:], invb[:], ALU.mult))
                ve_swp[kt] = ve
                p.wait("sync", "ve", ve)
                ds_fin[kt] = p.dma("sync", xout[kt * 128:(kt + 1) * 128, :], xo[:],
                                   inc="dxo")

        p.dma_done_waits("sync")

        with blk_cm as blk:
            p.finalize(blk)
    return nc



L = 4096


def build_attn_launch(dbg=False):
    nc = bass.Bass("TRN2", target_bir_lowering=False, detect_race_conditions=not dbg)
    st = contextlib.ExitStack()
    with st:
        v2f = nc.dram_tensor("v2f", [D, L], BF16, kind="ExternalInput").ap()
        winT_d = nc.dram_tensor("winT", [D, 512], F32, kind="ExternalInput").ap()
        w1T_d = nc.dram_tensor("w1T", [4, 512, 2048], F32, kind="ExternalInput").ap()
        w2T_d = nc.dram_tensor("w2T", [4, 2048, 512], F32, kind="ExternalInput").ap()
        woutT_d = nc.dram_tensor("woutT", [512, D], F32, kind="ExternalInput").ap()
        svec_d = nc.dram_tensor("svec", [128, 160], F32, kind="ExternalInput").ap()
        b1b_d = nc.dram_tensor("b1b", [128, 64], F32, kind="ExternalInput").ap()
        binb_d = nc.dram_tensor("binb", [128, 4], F32, kind="ExternalInput").ap()
        lnlb_d = nc.dram_tensor("lnlb", [128, 8], F32, kind="ExternalInput").ap()
        boutb_d = nc.dram_tensor("boutb", [128, 32], F32, kind="ExternalInput").ap()
        rwT_d = nc.dram_tensor("rwT", [D, 8], F32, kind="ExternalInput").ap()
        qmn_d = nc.dram_tensor("qmn", [1, L], F32, kind="ExternalInput").ap()
        v3o = nc.dram_tensor("v3o", [D, L], BF16, kind="ExternalOutput").ap()
        mu3o = nc.dram_tensor("mu3o", [1, L], F32, kind="ExternalOutput").ap()
        inv3o = nc.dram_tensor("inv3o", [1, L], F32, kind="ExternalOutput").ap()
        lparto = nc.dram_tensor("lparto", [8, 1], F32, kind="ExternalOutput").ap()
        v3scr = nc.dram_tensor("v3scr", [D, L], BF16, kind="Internal").ap()
        if dbg:
            dbg1 = nc.dram_tensor("dbg1", [128, 4 * L], F32, kind="ExternalOutput").ap()
            dbg2 = nc.dram_tensor("dbg2", [128, 16 * L], F32, kind="ExternalOutput").ap()
            dbg3 = nc.dram_tensor("dbg3", [128, 2 * L], BF16, kind="ExternalOutput").ap()
            dbg4 = nc.dram_tensor("dbg4", [128, 8], F32, kind="ExternalOutput").ap()
            dbg5 = nc.dram_tensor("dbg5", [128, 4 * L], F32, kind="ExternalOutput").ap()
            dbg6 = nc.dram_tensor("dbg6", [128, 16 * L], F32, kind="ExternalOutput").ap()
            dbg7 = nc.dram_tensor("dbg7", [128, 64], F32, kind="ExternalOutput").ap()
        mu2scr = nc.dram_tensor("mu2scr", [1, L], F32, kind="Internal").ap()
        c1scr = nc.dram_tensor("c1scr", [1, 2048], F32, kind="Internal").ap()
        coutscr = nc.dram_tensor("coutscr", [1, 4096], F32, kind="Internal").ap()
        inv2scr = nc.dram_tensor("inv2scr", [1, L], F32, kind="Internal").ap()

        h = st.enter_context(nc.sbuf_tensor("h", [128, 4 * L], F32))
        S = st.enter_context(nc.sbuf_tensor("S", [128, 4 * L], BF16))
        SCR = st.enter_context(nc.sbuf_tensor("SCR", [128, 32768], BF16))
        SCRf = SCR[:].bitcast(F32)
        Sf = S[:].bitcast(F32)
        rowA = st.enter_context(nc.sbuf_tensor("rowA", [1, 2048], F32))
        rowB = st.enter_context(nc.sbuf_tensor("rowB", [1, 2048], F32))
        pb1 = st.enter_context(nc.sbuf_tensor("pb1", [128, L], BF16))
        pb2 = st.enter_context(nc.sbuf_tensor("pb2", [128, L], F32))
        colb = st.enter_context(nc.sbuf_tensor("colb", [128, 56], F32))
        svb = st.enter_context(nc.sbuf_tensor("svb", [128, 160], F32))
        b1b = st.enter_context(nc.sbuf_tensor("b1bs", [128, 64], F32))
        binb = st.enter_context(nc.sbuf_tensor("binbs", [128, 4], F32))
        lnlb = st.enter_context(nc.sbuf_tensor("lnlbs", [128, 8], F32))
        boutb = st.enter_context(nc.sbuf_tensor("boutbs", [128, 32], F32))
        rwbf = st.enter_context(nc.sbuf_tensor("rwbf", [128, 256], BF16))
        onesb = st.enter_context(nc.sbuf_tensor("onesb", [128, 1], BF16))
        ones1 = st.enter_context(nc.sbuf_tensor("ones1", [1, 128], F32))
        winch = [st.enter_context(nc.sbuf_tensor(f"winch{i}", [128, 512], BF16))
                 for i in range(2)]
        negws = st.enter_context(nc.sbuf_tensor("negws", [128, 4], F32))
        b1c = st.enter_context(nc.sbuf_tensor("b1c", [128, 16], F32))
        lnlcolb = st.enter_context(nc.sbuf_tensor("lnlcolb", [128, 8], BF16))
        coutb = st.enter_context(nc.sbuf_tensor("coutb", [128, 32], F32))
        sm72 = st.enter_context(nc.sbuf_tensor("sm72", [72, 8], F32))
        smr = st.enter_context(nc.sbuf_tensor("smr", [1, 8], F32))
        PM = st.enter_context(nc.psum_tensor("PMc", [128, 2048], F32))
        PM2 = st.enter_context(nc.psum_tensor("PM2c", [128, 2048], F32))

        yF = SCRf[:, 0:4096]
        uF = SCRf[:, 4096:8192]
        vF = SCRf[:, 8192:12288]
        T1f = Sf[:, 0:4096]
        T2f = Sf[:, 4096:8192]
        rhoP = SCRf[:, 0:2048]
        x2e = [SCR[:, 0:16384], SCR[:, 16384:32768]]
        w1bf = SCR[:, 0:8192]
        w2bf = SCR[:, 8192:16384]
        zbf = [SCR[:, 16384 + i * 512:16384 + (i + 1) * 512] for i in range(2)]
        woutbf = SCR[:, 0:16384]
        v2rd = [SCR[:, 16384 + i * 2048:16384 + (i + 1) * 2048] for i in range(2)]
        xres = SCRf[:, 14336:16384]
        projv = [SCRf[64:72, 12288:14336], SCRf[64:72, 14336:16384]]
        t8s = SCRf[64:72, 0:2048]
        vbf = [SCR[:, 20480:22528], SCR[:, 22528:24576]]

        p = Plan(nc, st)
        p.sem("dg", "ds", "pe", "ve", "se")
        blk_cm = nc.Block()

        p.op("vector", lambda v: v.memset(onesb[:], 1.0), inc="ve")
        p.op("vector", lambda v: v.memset(ones1[:], 1.0), inc="ve")
        if dbg:
            p.op("vector", lambda v: v.memset(colb[:], 0.0), inc="ve")
            p.op("vector", lambda v: v.memset(sm72[:], 0.0), inc="ve")
        ve_const = p.c["ve"]

        p.dma("sync", svb[:], svec_d, inc="dcon")
        p.dma("sync", b1b[:], b1b_d, inc="dcon")
        p.dma("sync", binb[:], binb_d, inc="dcon")
        p.dma("sync", lnlb[:], lnlb_d, inc="dcon")
        p.dma("sync", boutb[:], boutb_d, inc="dcon")
        ds_blobs = p.dma("gpsimd", rwbf[:].rearrange("p (kt e) -> p kt e", kt=KT),
                         rwT_d.rearrange("(kt p) e -> p kt e", p=128), inc="dff")
        ds_blobs_s = p.c["dcon"]
        p.wait("vector", "dcon", p.c["dcon"])
        p.wait("scalar", "dcon", p.c["dcon"])
        p.wait("vector", "dff", ds_blobs)

        def svcol(i, v, g=None):
            base = (i * 10 + v) * 4
            if g is None:
                return svb[:, base:base + 4]
            return svb[:, base + g:base + g + 1]

        def ln_stats(tag):
            p.wait("tensor", "ve", p.c["ve"])
            p.wait("vector", "pe", p.c["pe"])
            first = True
            for g in range(4):
                if first:
                    p.seq("vector", lambda v, g=g: v.tensor_copy(S[:, g * L:(g + 1) * L],
                                                                 h[:, g * L:(g + 1) * L]))
                    first = False
                else:
                    p.op("vector", lambda v, g=g: v.tensor_copy(S[:, g * L:(g + 1) * L],
                                                                h[:, g * L:(g + 1) * L]),
                         inc="ve")
            ve_cast = p.c["ve"]
            for half in range(2):
                c0 = half * 2048
                p.wait("tensor", "ve", p.c["ve"])
                for g in range(4):
                    p.wait("scalar", "ve", ve_cast)
                    if g >= 1:
                        p.wait("scalar", "pe", p.c["pe"])
                    se = p.op("scalar", lambda s, g=g, c0=c0: s.activation(
                        SCR[:, 0:2048], S[:, g * L + c0: g * L + c0 + 2048], AF.Square),
                        inc="se")
                    p.wait("tensor", "se", se)
                    for c in range(4):
                        p.op("tensor", lambda t, g=g, c0=c0, c=c: t.matmul(
                            PM[0:1, c * 512:(c + 1) * 512], onesb[:],
                            S[:, g * L + c0 + c * 512: g * L + c0 + (c + 1) * 512],
                            start=(g == 0), stop=(g == 3)), inc="pe")
                        p.op("tensor", lambda t, g=g, c=c: t.matmul(
                            PM2[0:1, c * 512:(c + 1) * 512], onesb[:],
                            SCR[:, 0:2048][:, c * 512:(c + 1) * 512],
                            start=(g == 0), stop=(g == 3)), inc="pe")
                p.wait("vector", "pe", p.c["pe"])
                finalize_512(PM[0:1, 0:2048], PM2[0:1, 0:2048], 2048)
                bc(rowA, pb1[:, c0:c0 + 2048])
                bc(rowB, pb2[:, c0:c0 + 2048])

        def finalize_512(sum_ps, sq_ps, W, nf=512):
            if p.c.get("dmu", 0) or p.c.get("dsr", 0):
                p.wait("vector", "dmu", p.c.get("dmu", 0))
                p.wait("vector", "dsr", p.c.get("dsr", 0))
            p.seq("vector", lambda v: v.tensor_scalar_mul(rowA[0:1, 0:W], sum_ps, 1.0 / nf))
            p.seq("vector", lambda v: v.tensor_scalar_mul(rowB[0:1, 0:W], sq_ps, 1.0 / nf))
            p.seq("vector", lambda v: v.tensor_tensor(sum_ps, rowA[0:1, 0:W], rowA[0:1, 0:W],
                                                      ALU.mult))
            p.seq("vector", lambda v: v.tensor_tensor(rowB[0:1, 0:W], rowB[0:1, 0:W], sum_ps,
                                                      ALU.subtract))
            ve = p.seq("vector", lambda v: v.tensor_scalar(rowB[0:1, 0:W], rowB[0:1, 0:W], EPS,
                                                           None, ALU.add))
            p.wait("scalar", "ve", ve)
            se1 = p.op("scalar", lambda s: s.activation(rowB[0:1, 0:W], rowB[0:1, 0:W], AF.Ln),
                       inc="se")
            p.wait("vector", "se", se1)
            ve2 = p.seq("vector", lambda v: v.tensor_scalar(rowB[0:1, 0:W], rowB[0:1, 0:W],
                                                            -0.5, None, ALU.mult))
            p.wait("scalar", "ve", ve2)
            se2 = p.op("scalar", lambda s: s.activation(rowB[0:1, 0:W], rowB[0:1, 0:W], AF.Exp),
                       inc="se")
            p.wait("vector", "se", se2)
            return p.c["se"]

        def bc(row, plane):
            W = plane.shape[1]
            p.wait("tensor", "ve", p.c["ve"])
            p.wait("tensor", "se", p.c["se"])
            for cidx in range((W + 511) // 512):
                c0, c1 = cidx * 512, min((cidx + 1) * 512, W)
                pe = p.op("tensor", lambda t, row=row, c0=c0, c1=c1: t.matmul(
                    PM[:, c0:c1], ones1[:], row[0:1, c0:c1], start=True, stop=True), inc="pe")
                p.wait("vector", "pe", pe)
                p.op("vector", lambda v, plane=plane, c0=c0, c1=c1: v.tensor_copy(
                    plane[:, c0:c1], PM[:, c0:c1]), inc="ve")
            return p.c["ve"]

        v2f3 = v2f.rearrange("(kt p) m -> p kt m", p=128)
        for kt in range(KT):
            if kt >= 2:
                p.wait("gpsimd", "pe", p.c["pe"])
            dsw0 = p.dma("gpsimd", winch[kt % 2][:], winT_d[kt * 128:(kt + 1) * 128, :],
                         inc=f"dwi{kt % 2}")
            p.wait("tensor", f"dwi{kt % 2}", dsw0)
            if kt == 0:
                p.wait("tensor", "ve", ve_const)
            for dt in range(4):
                p.op("tensor", lambda t, kt=kt, dt=dt: t.matmul(
                    PM[:, dt * 512:dt * 512 + 1],
                    winch[kt % 2][:, dt * 128:(dt + 1) * 128], onesb[:],
                    start=(kt == 0), stop=(kt == KT - 1)), inc="pe")
        pe_ws = p.c["pe"]
        p.wait("vector", "pe", pe_ws)
        for dt in range(4):
            p.op("vector", lambda v, dt=dt: v.tensor_scalar(
                negws[:, dt:dt + 1], PM[:, dt * 512:dt * 512 + 1], -1.0, None, ALU.mult),
                inc="ve")
        ve_negws = p.c["ve"]
        ve_evac_e = [0] * 8
        dg_x2 = [0] * 8
        pe_win_kt = {}
        pe_sq_kt = {}
        for e in range(8):
            if e >= 2:
                p.wait("gpsimd", "pe", pe_win_kt[(e - 2, KT - 1)])
            for i2 in range(4):
                dg_x2[e] = p.dma("gpsimd",
                                 x2e[e % 2].rearrange("p (kt m) -> p kt m", kt=KT)
                                 [:, 8 * i2:8 * i2 + 8, :],
                                 v2f3[:, 8 * i2:8 * i2 + 8, e * 512:(e + 1) * 512],
                                 inc=f"dxa{e % 2}")
            for kt in range(KT):
                if kt >= 2:
                    p.wait("gpsimd", "pe", pe_win_kt[(e, kt - 2)])
                elif e >= 1:
                    p.wait("gpsimd", "pe", pe_win_kt[(e - 1, KT - 2 + kt)])
                else:
                    p.wait("gpsimd", "pe", pe_ws)
                dsw = p.dma("gpsimd", winch[kt % 2][:],
                            winT_d[kt * 128:(kt + 1) * 128, :], inc=f"dwi{kt % 2}")
                p.wait("tensor", f"dwi{kt % 2}", dsw)
                p.wait("tensor", f"dxa{e % 2}", dg_x2[e])
                if kt == 0:
                    p.wait("tensor", "ve", ve_negws if e == 0 else ve_evac_e[e - 1])
                xtile = x2e[e % 2][:, kt * 512:(kt + 1) * 512]
                p.op("tensor", lambda t, kt=kt, xtile=xtile: t.matmul(
                    PM2[0:1, 0:512], onesb[:], xtile, start=(kt == 0), stop=(kt == KT - 1)),
                    inc="pe")
                p.wait("scalar", f"dxa{e % 2}", dg_x2[e])
                if e >= 1 or kt >= 1:
                    p.wait("scalar", "pe", pe_sq_kt[(e, kt - 1) if kt >= 1 else
                                                    (e - 1, KT - 1)])
                sq_se = p.op("scalar", lambda s, xtile=xtile: s.activation(
                    S[:, 0:512], xtile, AF.Square), inc="se")
                p.wait("tensor", "se", sq_se)
                pe_sq_kt[(e, kt)] = p.op("tensor", lambda t, kt=kt: t.matmul(
                    PM2[0:1, 512:1024], onesb[:], S[:, 0:512],
                    start=(kt == 0), stop=(kt == KT - 1)), inc="pe")
                for dt in range(4):
                    p.op("tensor", lambda t, kt=kt, dt=dt, xtile=xtile: t.matmul(
                        PM[:, dt * 512:(dt + 1) * 512],
                        winch[kt % 2][:, dt * 128:(dt + 1) * 128], xtile,
                        start=(kt == 0), stop=(kt == KT - 1)), inc="pe")
                pe_win_kt[(e, kt)] = p.c["pe"]
            p.wait("vector", "pe", pe_win_kt[(e, KT - 1)])
            finalize_512(PM2[0:1, 0:512], PM2[0:1, 512:1024], 512, nf=D)
            p.wait("sync", "ve", p.c["ve"])
            p.wait("sync", "se", p.c["se"])
            p.dma("sync", mu2scr[0:1, e * 512:(e + 1) * 512], rowA[0:1, 0:512], inc="dmu")
            ds_mi = p.dma("sync", inv2scr[0:1, e * 512:(e + 1) * 512], rowB[0:1, 0:512],
                          inc="dmu")
            p.wait("vector", "dmu", ds_mi)
            p.seq("vector", lambda v: v.tensor_tensor(rowA[0:1, 0:512], rowA[0:1, 0:512],
                                                      rowB[0:1, 0:512], ALU.mult))
            for row, plane in ((rowB, pb2), (rowA, pb1)):
                p.wait("tensor", "ve", p.c["ve"])
                p.wait("tensor", "se", p.c["se"])
                pe = p.op("tensor", lambda t, row=row: t.matmul(
                    PM2[:, 0:512], ones1[:], row[0:1, 0:512], start=True, stop=True), inc="pe")
                p.wait("vector", "pe", pe)
                p.op("vector", lambda v, plane=plane, e=e: v.tensor_copy(
                    plane[:, e * 512:(e + 1) * 512], PM2[:, 0:512]), inc="ve")
            for dt in range(4):
                p.op("vector", lambda v, dt=dt, e=e: v.tensor_tensor(
                    h[:, dt * L + e * 512: dt * L + (e + 1) * 512],
                    PM[:, dt * 512:(dt + 1) * 512], pb2[:, e * 512:(e + 1) * 512], ALU.mult),
                    inc="ve")
                p.seq("vector", lambda v, dt=dt, e=e: v.scalar_tensor_tensor(
                    h[:, dt * L + e * 512: dt * L + (e + 1) * 512],
                    pb1[:, e * 512:(e + 1) * 512], negws[:, dt:dt + 1],
                    h[:, dt * L + e * 512: dt * L + (e + 1) * 512], ALU.mult, ALU.add))
                p.seq("vector", lambda v, dt=dt, e=e: v.tensor_scalar(
                    h[:, dt * L + e * 512: dt * L + (e + 1) * 512],
                    h[:, dt * L + e * 512: dt * L + (e + 1) * 512],
                    binb[:, dt:dt + 1], None, ALU.add))
            ve_evac_e[e] = p.c["ve"]

        if dbg:
            p.wait("sync", "ve", ve_evac_e[7])
            p.dma("sync", dbg1, h[:], inc="ddbg")
            p.dma("sync", dbg3[:, 0:L], pb1[:], inc="ddbg")
            pass
            p.dma("sync", dbg4[:, 0:4], negws[:], inc="ddbg")
            dsd0 = p.dma("sync", dbg4[:, 4:8], colb[:, 16:20], inc="ddbg")
            p.wait("vector", "ddbg", dsd0)
            p.wait("scalar", "ddbg", dsd0)

        for it in range(4):
            ln_stats("ln1")
            for g in range(4):
                cc = colb
                prv, piv = svcol(it, 2), svcol(it, 3)
                lrv, liv = svcol(it, 4), svcol(it, 5)
                p.seq("vector", lambda v, a=prv: v.tensor_tensor(cc[:, 0:4], a, a, ALU.mult))
                p.seq("vector", lambda v, a=piv: v.tensor_tensor(cc[:, 4:8], a, a, ALU.mult))
                p.seq("vector", lambda v: v.tensor_tensor(cc[:, 0:4], cc[:, 0:4], cc[:, 4:8],
                                                          ALU.add))
                p.seq("vector", lambda v: v.tensor_scalar(cc[:, 0:4], cc[:, 0:4], 1e-38, None,
                                                          ALU.max))
                ve = p.c["ve"]
                p.wait("scalar", "ve", ve)
                se = p.op("scalar", lambda s: s.activation(cc[:, 4:8], cc[:, 0:4], AF.Sqrt),
                          inc="se")
                p.wait("vector", "se", se)
                p.seq("vector", lambda v: v.reciprocal(cc[:, 8:12], cc[:, 4:8]))
                p.seq("vector", lambda v: v.tensor_tensor(cc[:, 12:16], cc[:, 0:4], cc[:, 8:12],
                                                          ALU.mult))
                p.seq("vector", lambda v: v.tensor_tensor(cc[:, 12:16], cc[:, 12:16], cc[:, 4:8],
                                                          ALU.add))
                p.seq("vector", lambda v: v.tensor_scalar(cc[:, 12:16], cc[:, 12:16], 0.5, None,
                                                          ALU.mult))
                ve = p.c["ve"]
                p.wait("scalar", "ve", ve)
                se = p.op("scalar", lambda s: s.activation(cc[:, 16:20], cc[:, 12:16], AF.Tanh),
                          inc="se")
                p.wait("vector", "se", se)
                p.seq("vector", lambda v: v.reciprocal(cc[:, 20:24], cc[:, 12:16]))
                p.seq("vector", lambda v, a=prv: v.tensor_tensor(cc[:, 24:28], a, cc[:, 20:24],
                                                                 ALU.mult))
                p.seq("vector", lambda v, a=piv: v.tensor_tensor(cc[:, 28:32], a, cc[:, 20:24],
                                                                 ALU.mult))
                p.seq("vector", lambda v, a=lrv: v.tensor_tensor(cc[:, 32:36], cc[:, 24:28], a,
                                                                 ALU.mult))
                p.seq("vector", lambda v, a=liv: v.tensor_tensor(cc[:, 36:40], cc[:, 28:32], a,
                                                                 ALU.mult))
                p.seq("vector", lambda v: v.tensor_tensor(cc[:, 32:36], cc[:, 32:36],
                                                          cc[:, 36:40], ALU.subtract))
                p.seq("vector", lambda v, a=liv: v.tensor_tensor(cc[:, 36:40], cc[:, 24:28], a,
                                                                 ALU.mult))
                p.seq("vector", lambda v, a=lrv: v.tensor_tensor(cc[:, 40:44], cc[:, 28:32], a,
                                                                 ALU.mult))
                p.seq("vector", lambda v: v.tensor_tensor(cc[:, 36:40], cc[:, 36:40],
                                                          cc[:, 40:44], ALU.add))
                p.seq("vector", lambda v: v.tensor_scalar(cc[:, 36:40], cc[:, 36:40], -1.0,
                                                          None, ALU.mult))
                for half in range(2):
                    c0 = half * 2048
                    p.seq("vector", lambda v, g=g, c0=c0: v.tensor_tensor(
                        yF[:, c0:c0 + 2048], h[:, g * L + c0: g * L + c0 + 2048],
                        pb1[:, c0:c0 + 2048], ALU.subtract))
                    ve = p.seq("vector", lambda v, g=g, c0=c0: v.tensor_tensor(
                        yF[:, c0:c0 + 2048], yF[:, c0:c0 + 2048], pb2[:, c0:c0 + 2048],
                        ALU.mult))
                    p.wait("scalar", "ve", ve)
                    p.op("scalar", lambda s, g=g, c0=c0, it=it: s.activation(
                        yF[:, c0:c0 + 2048], yF[:, c0:c0 + 2048], AF.Silu,
                        bias=svcol(it, 1, g), scale=svcol(it, 0, g)), inc="se")
                se_y = p.c["se"]
                if dbg and it == 0 and g == 0:
                    p.wait("sync", "se", p.c["se"])
                    p.wait("sync", "ve", p.c["ve"])
                    p.dma("sync", dbg5[:, 0:L], yF[:], inc="ddbg")
                    dsd = p.dma("sync", dbg7[:, 0:44], colb[:, 0:44], inc="ddbg")
                    p.wait("vector", "ddbg", dsd)
                    p.wait("scalar", "ddbg", dsd)
                tscr = [uF[:, 0:1024], vF[:, 0:1024]]
                p.wait("vector", "se", se_y)
                p.op("vector", lambda v: v.memset(T1f[:, 0:1], 1.0), inc="ve")
                p.op("vector", lambda v: v.memset(T2f[:, 0:1], 0.0), inc="ve")
                p.seq("vector", lambda v, g=g: v.tensor_copy(T1f[:, 1:2], cc[:, 24 + g:25 + g]))
                p.seq("vector", lambda v, g=g: v.tensor_copy(T2f[:, 1:2], cc[:, 28 + g:29 + g]))
                p.seq("vector", lambda v, g=g: v.tensor_copy(cc[:, 44:45], cc[:, 24 + g:25 + g]))
                p.seq("vector", lambda v, g=g: v.tensor_copy(cc[:, 45:46], cc[:, 28 + g:29 + g]))
                if dbg and it == 0 and g == 0:
                    p.wait("sync", "ve", p.c["ve"])
                    dsdi = p.dma("sync", dbg7[:, 20:28], colb[:, 24:32], inc="ddbg")
                    dsdi = p.dma("sync", dbg7[:, 28:30], colb[:, 44:46], inc="ddbg")
                    p.wait("vector", "ddbg", dsdi)
                s_sz = 2
                while s_sz <= 2048:
                    if s_sz > 2:
                        p.seq("vector", lambda v: v.tensor_tensor(cc[:, 46:47], cc[:, 44:45],
                                                                  cc[:, 44:45], ALU.mult))
                        p.seq("vector", lambda v: v.tensor_tensor(cc[:, 47:48], cc[:, 45:46],
                                                                  cc[:, 45:46], ALU.mult))
                        p.seq("vector", lambda v: v.tensor_tensor(cc[:, 48:49], cc[:, 44:45],
                                                                  cc[:, 45:46], ALU.mult))
                        p.seq("vector", lambda v: v.tensor_tensor(cc[:, 44:45], cc[:, 46:47],
                                                                  cc[:, 47:48], ALU.subtract))
                        p.seq("vector", lambda v: v.tensor_scalar(cc[:, 45:46], cc[:, 48:49],
                                                                  2.0, None, ALU.mult))
                    else:
                        p.seq("vector", lambda v: v.tensor_tensor(cc[:, 46:47], cc[:, 44:45],
                                                                  cc[:, 44:45], ALU.mult))
                        p.seq("vector", lambda v: v.tensor_tensor(cc[:, 47:48], cc[:, 45:46],
                                                                  cc[:, 45:46], ALU.mult))
                        p.seq("vector", lambda v: v.tensor_tensor(cc[:, 48:49], cc[:, 44:45],
                                                                  cc[:, 45:46], ALU.mult))
                        p.seq("vector", lambda v: v.tensor_tensor(cc[:, 44:45], cc[:, 46:47],
                                                                  cc[:, 47:48], ALU.subtract))
                        p.seq("vector", lambda v: v.tensor_scalar(cc[:, 45:46], cc[:, 48:49],
                                                                  2.0, None, ALU.mult))
                    p.seq("vector", lambda v: v.tensor_scalar(cc[:, 49:50], cc[:, 45:46], -1.0,
                                                              None, ALU.mult))
                    for j0 in range(0, s_sz, 1024):
                        ln_c = min(1024, s_sz - j0)
                        p.seq("vector", lambda v, j0=j0, ln_c=ln_c, s=s_sz: v.tensor_scalar(
                            tscr[0][:, 0:ln_c], T1f[:, j0:j0 + ln_c], cc[:, 44:45], None,
                            ALU.mult))
                        p.seq("vector", lambda v, j0=j0, ln_c=ln_c, s=s_sz: v.tensor_scalar(
                            tscr[1][:, 0:ln_c], T1f[:, j0:j0 + ln_c], cc[:, 45:46], None,
                            ALU.mult))
                        p.seq("vector", lambda v, j0=j0, ln_c=ln_c, s=s_sz:
                              v.scalar_tensor_tensor(
                            T1f[:, s + j0:s + j0 + ln_c], T2f[:, j0:j0 + ln_c], cc[:, 49:50],
                            tscr[0][:, 0:ln_c], ALU.mult, ALU.add))
                        p.seq("vector", lambda v, j0=j0, ln_c=ln_c, s=s_sz:
                              v.scalar_tensor_tensor(
                            T2f[:, s + j0:s + j0 + ln_c], T2f[:, j0:j0 + ln_c], cc[:, 44:45],
                            tscr[1][:, 0:ln_c], ALU.mult, ALU.add))
                    s_sz *= 2
                if dbg and it == 0 and g == 0:
                    p.wait("sync", "ve", p.c["ve"])
                    dsd = p.dma("sync", dbg5[:, L:2 * L], T1f[:], inc="ddbg")
                    dsd = p.dma("sync", dbg5[:, 2 * L:3 * L], T2f[:], inc="ddbg")
                    p.wait("vector", "ddbg", dsd)
                p.seq("vector", lambda v: v.tensor_tensor(uF[:], T1f[:], yF[:], ALU.mult))
                p.seq("vector", lambda v: v.tensor_tensor(vF[:], T2f[:], yF[:], ALU.mult))
                p.seq("vector", lambda v, g=g: v.tensor_scalar(rhoP[:], T1f[:, 0:2048], 0.0,
                                                               cc[:, 16 + g:17 + g], ALU.mult,
                                                               ALU.add))
                p.seq("vector", lambda v, g=g: v.tensor_tensor_scan(uF[:, 0:2048], rhoP[:],
                                                                    uF[:, 0:2048],
                                                                    cc[:, 32 + g:33 + g],
                                                                    ALU.mult, ALU.add))
                p.seq("vector", lambda v: v.tensor_tensor_scan(uF[:, 2048:4096], rhoP[:],
                                                               uF[:, 2048:4096],
                                                               uF[:, 2047:2048],
                                                               ALU.mult, ALU.add))
                p.seq("vector", lambda v, g=g: v.tensor_tensor_scan(vF[:, 0:2048], rhoP[:],
                                                                    vF[:, 0:2048],
                                                                    cc[:, 36 + g:37 + g],
                                                                    ALU.mult, ALU.add))
                p.seq("vector", lambda v: v.tensor_tensor_scan(vF[:, 2048:4096], rhoP[:],
                                                               vF[:, 2048:4096],
                                                               vF[:, 2047:2048],
                                                               ALU.mult, ALU.add))
                if dbg and it == 0 and g == 0:
                    p.wait("sync", "ve", p.c["ve"])
                    dsd = p.dma("sync", dbg5[:, 3 * L:4 * L], uF[:], inc="ddbg")
                    p.wait("vector", "ddbg", dsd)
                p.seq("vector", lambda v: v.tensor_tensor(yF[:], T1f[:], uF[:], ALU.mult))
                p.seq("vector", lambda v: v.tensor_tensor(vF[:], T2f[:], vF[:], ALU.mult))
                p.seq("vector", lambda v: v.tensor_tensor(yF[:], yF[:], vF[:], ALU.add))
                p.seq("vector", lambda v, g=g, it=it: v.scalar_tensor_tensor(
                    h[:, g * L:(g + 1) * L], yF[:], svcol(it, 6, g), h[:, g * L:(g + 1) * L],
                    ALU.mult, ALU.add))
            if dbg:
                p.wait("sync", "ve", p.c["ve"])
                dsd = p.dma("sync", dbg6[:, it * 4 * L:(it + 1) * 4 * L], h[:], inc="ddbg")
                p.wait("vector", "ddbg", dsd)
            ln_stats("ln2_w")
            for g in range(4):
                for half in range(2):
                    c0 = half * 2048
                    p.seq("vector", lambda v, g=g, c0=c0: v.tensor_tensor(
                        yF[:, 0:2048], h[:, g * L + c0: g * L + c0 + 2048],
                        pb1[:, c0:c0 + 2048], ALU.subtract))
                    p.seq("vector", lambda v, g=g, c0=c0: v.tensor_tensor(
                        S[:, g * L + c0: g * L + c0 + 2048], yF[:, 0:2048],
                        pb2[:, c0:c0 + 2048], ALU.mult))
            ve_y2 = p.c["ve"]
            p.wait("gpsimd", "ve", ve_y2)
            dgw1 = p.dma("gpsimd", w1bf.rearrange("p (dt f) -> p dt f", dt=4),
                         w1T_d[it].rearrange("(dt p) f -> p dt f", p=128), inc="dff")
            dgw2 = p.dma("gpsimd", w2bf.rearrange("p (ft d2) -> p ft d2", ft=16),
                         w2T_d[it].rearrange("(ft p) d2 -> p ft d2", p=128), inc="dff")
            p.wait("vector", "dff", dgw2)
            for dt in range(4):
                p.seq("vector", lambda v, dt=dt, it=it: v.tensor_scalar(
                    w1bf[:, dt * 2048:(dt + 1) * 2048], w1bf[:, dt * 2048:(dt + 1) * 2048],
                    svcol(it, 7, dt), None, ALU.mult))
            ve_w1 = p.c["ve"]
            for g in range(4):
                p.op("vector", lambda v, g=g, it=it: v.tensor_copy(
                    lnlcolb[:, 4 + g:5 + g], svcol(it, 8, g)), inc="ve")
            ve_l2b = p.c["ve"]
            p.wait("tensor", "ve", max(ve_w1, ve_l2b))
            for c in range(4):
                for dt in range(4):
                    p.op("tensor", lambda t, c=c, dt=dt: t.matmul(
                        PM2[0:1, c * 512:(c + 1) * 512],
                        lnlcolb[:, 4 + dt:5 + dt],
                        w1bf[:, dt * 2048 + c * 512: dt * 2048 + (c + 1) * 512],
                        start=(dt == 0), stop=(dt == 3)), inc="pe")
            pe_c1 = p.c["pe"]
            p.wait("vector", "pe", pe_c1)
            ve_c1r = p.op("vector", lambda v: v.tensor_copy(rowA[0:1, 0:2048],
                                                            PM2[0:1, 0:2048]), inc="ve")
            p.wait("sync", "ve", ve_c1r)
            p.dma("sync", c1scr, rowA[0:1, 0:2048], inc="dsr")
            p.wait("sync", "dsr", p.c["dsr"])
            dsc1 = p.dma("sync", b1c[:], c1scr.rearrange("one (ftc p) -> p ftc", p=128),
                         inc="dsr", allow_slow_non_contiguous=True)
            p.wait("vector", "dsr", dsc1)
            ve_b1c = p.seq("vector", lambda v, it=it: v.tensor_tensor(
                b1c[:], b1c[:], b1b[:, it * 16:(it + 1) * 16], ALU.add))
            se_z = {}
            pe_s2_last = 0
            ve_ffn_evac = {}
            for lc in range(8):
                c0 = lc * 512
                for ft in range(16):
                    reg = (ft % 2) * 512
                    p.wait("tensor", "ve", max(ve_y2, ve_b1c) if (lc == 0 and ft == 0)
                           else 0)
                    p.wait("tensor", "se", se_z.get((lc, ft - 2), 0))
                    for dt in range(4):
                        p.op("tensor", lambda t, lc=lc, ft=ft, dt=dt, reg=reg, c0=c0: t.matmul(
                            PM2[:, reg:reg + 512],
                            w1bf[:, dt * 2048 + ft * 128: dt * 2048 + (ft + 1) * 128],
                            S[:, dt * L + c0: dt * L + c0 + 512],
                            start=(dt == 0), stop=(dt == 3)), inc="pe")
                    pe_s1 = p.c["pe"]
                    p.wait("scalar", "pe", pe_s1)
                    p.wait("scalar", "ve", ve_b1c)
                    if (lc, ft - 2) in se_z:
                        p.wait("scalar", "pe", pe_s2_last)
                    se_z[(lc, ft)] = p.op("scalar", lambda s, ft=ft, reg=reg: s.activation(
                        zbf[ft % 2][:], PM2[:, reg:reg + 512], AF.Silu,
                        bias=b1c[:, ft:ft + 1]), inc="se")
                    p.wait("tensor", "se", se_z[(lc, ft)])
                    if ft == 0 and lc >= 1:
                        p.wait("tensor", "ve", ve_ffn_evac[lc - 1])
                    for dt in range(4):
                        p.op("tensor", lambda t, lc=lc, ft=ft, dt=dt: t.matmul(
                            PM[:, dt * 512:(dt + 1) * 512],
                            w2bf[:, ft * 512 + dt * 128: ft * 512 + (dt + 1) * 128],
                            zbf[ft % 2][:], start=(ft == 0), stop=(ft == 15)), inc="pe")
                    pe_s2_last = p.c["pe"]
                p.wait("vector", "pe", pe_s2_last)
                for dt in range(4):
                    p.op("vector", lambda v, lc=lc, dt=dt, it=it, c0=c0: v.scalar_tensor_tensor(
                        h[:, dt * L + c0: dt * L + c0 + 512], PM[:, dt * 512:(dt + 1) * 512],
                        svcol(it, 9, dt), h[:, dt * L + c0: dt * L + c0 + 512],
                        ALU.add, ALU.add), inc="ve")
                ve_ffn_evac[lc] = p.c["ve"]
            if dbg:
                p.wait("sync", "ve", p.c["ve"])
                dsd = p.dma("sync", dbg2[:, it * 4 * L:(it + 1) * 4 * L], h[:], inc="ddbg")
                p.wait("vector", "ddbg", dsd)

        ln_stats("lnl")
        for g in range(4):
            for half in range(2):
                c0 = half * 2048
                p.seq("vector", lambda v, g=g, c0=c0: v.tensor_tensor(
                    yF[:, 0:2048], h[:, g * L + c0: g * L + c0 + 2048], pb1[:, c0:c0 + 2048],
                    ALU.subtract))
                p.seq("vector", lambda v, g=g, c0=c0: v.tensor_tensor(
                    S[:, g * L + c0: g * L + c0 + 2048], yF[:, 0:2048], pb2[:, c0:c0 + 2048],
                    ALU.mult))
        ve_y3 = p.c["ve"]
        p.wait("gpsimd", "ve", ve_y3)
        p.wait("gpsimd", "pe", p.c["pe"])
        dgw = p.dma("gpsimd", woutbf.rearrange("p (g vv) -> p g vv", g=4),
                    woutT_d.rearrange("(g p) vv -> p g vv", p=128), inc="dff")
        for g in range(4):
            p.seq("vector", lambda v, g=g: v.tensor_copy(lnlcolb[:, g:g + 1],
                                                         lnlb[:, 2 * g + 1:2 * g + 2]))
        ve_lnc = p.c["ve"]
        p.wait("tensor", "ve", ve_lnc)
        p.wait("tensor", "dff", p.c["dff"])
        for kt in range(KT):
            p.op("tensor", lambda t, kt=kt: t.matmul(
                PM[64:72, 0:1], rwbf[:, kt * 8:(kt + 1) * 8], onesb[:],
                start=(kt == 0), stop=(kt == KT - 1)), inc="pe")
        pe_rws = p.c["pe"]
        p.wait("vector", "pe", pe_rws)
        ve_rws = p.op("vector", lambda v: v.tensor_copy(sm72[64:72, 0:1], PM[64:72, 0:1]),
                      inc="ve")
        p.wait("tensor", "ve", ve_rws)
        for c in range(8):
            ps_c = PM if c < 4 else PM2
            cc_ = c % 4
            for g in range(4):
                p.op("tensor", lambda t, ps_c=ps_c, cc_=cc_, c=c, g=g: t.matmul(
                    ps_c[0:1, cc_ * 512:(cc_ + 1) * 512],
                    lnlcolb[:, g:g + 1],
                    woutbf[:, g * 4096 + c * 512: g * 4096 + (c + 1) * 512],
                    start=(g == 0), stop=(g == 3)), inc="pe")
        pe_cout = p.c["pe"]
        p.wait("vector", "pe", pe_cout)
        p.op("vector", lambda v: v.tensor_copy(rowA[0:1, 0:2048], PM[0:1, 0:2048]), inc="ve")
        ve_co = p.seq("vector", lambda v: v.tensor_copy(rowB[0:1, 0:2048], PM2[0:1, 0:2048]))
        p.wait("vector", "dff", dgw)
        for g in range(4):
            p.seq("vector", lambda v, g=g: v.tensor_scalar(
                woutbf[:, g * 4096:(g + 1) * 4096], woutbf[:, g * 4096:(g + 1) * 4096],
                lnlb[:, 2 * g:2 * g + 1], None, ALU.mult))
        p.wait("sync", "ve", ve_co)
        p.dma("sync", coutscr[0:1, 0:2048], rowA[0:1, 0:2048], inc="dsr")
        p.dma("sync", coutscr[0:1, 2048:4096], rowB[0:1, 0:2048], inc="dsr")
        p.wait("sync", "dsr", p.c["dsr"])
        dscb = p.dma("sync", coutb[:], coutscr.rearrange("one (vtc p) -> p vtc", p=128),
                     inc="dsr", allow_slow_non_contiguous=True)
        p.wait("vector", "dsr", dscb)
        cb = coutb
        ve_cb = p.seq("vector", lambda v: v.tensor_tensor(cb[:], cb[:], boutb[:], ALU.add))

        ds_v3 = {}
        for lh in range(2):
            c0 = lh * 2048
            p.wait("sync", "ve", p.c["ve"])
            p.dma("sync", rowA[0:1, 0:2048], mu2scr[0:1, c0:c0 + 2048], inc="dmu")
            dsr = p.dma("sync", rowB[0:1, 0:2048], inv2scr[0:1, c0:c0 + 2048], inc="dmu")
            p.wait("tensor", "dmu", dsr)
            p.wait("tensor", "ve", ve_cb if lh == 0 else p.c["ve"])
            bc(rowA, pb1[:, 0:2048])
            bc(rowB, pb2[:, 0:2048])
            ve_pb = p.c["ve"]
            pe_vt = [0] * KT
            ve_ev = [0] * KT
            for vt in range(KT):
                if vt >= 2:
                    p.wait("sync", "ve", ve_ev[vt - 2])
                dsv2 = p.dma("sync", v2rd[vt % 2],
                             v2f[vt * 128:(vt + 1) * 128, c0:c0 + 2048], inc=f"dra{vt % 2}")
                if vt == 0:
                    p.wait("tensor", "ve", max(ve_pb, ve_y3))
                if vt >= 1:
                    p.wait("tensor", "ve", ve_ev[vt - 1])
                for g in range(4):
                    for c in range(4):
                        p.op("tensor", lambda t, vt=vt, g=g, c=c, c0=c0: t.matmul(
                            PM[:, c * 512:(c + 1) * 512],
                            woutbf[:, g * 4096 + vt * 128: g * 4096 + (vt + 1) * 128],
                            S[:, g * L + c0 + c * 512: g * L + c0 + (c + 1) * 512],
                            start=(g == 0), stop=(g == 3)), inc="pe")
                pe_vt[vt] = p.c["pe"]
                p.wait("vector", "pe", pe_vt[vt])
                p.wait("vector", f"dra{vt % 2}", dsv2)
                if vt >= 2 or lh == 1:
                    p.wait("vector", f"dst{vt % 2}", ds_v3.get((lh, vt - 2),
                                                               ds_v3.get((lh - 1, KT - 2 + vt),
                                                                         0)))
                p.op("vector", lambda v, vt=vt: v.tensor_tensor(
                    xres[:], v2rd[vt % 2], pb1[:, 0:2048], ALU.subtract), inc="ve")
                p.seq("vector", lambda v: v.tensor_tensor(xres[:], xres[:], pb2[:, 0:2048],
                                                          ALU.mult))
                p.seq("vector", lambda v, vt=vt: v.scalar_tensor_tensor(
                    vbf[vt % 2], PM[0:128, 0:2048], cb[:, vt:vt + 1], xres[:],
                    ALU.add, ALU.add))
                ve_ev[vt] = p.c["ve"]
                p.wait("sync", "ve", ve_ev[vt])
                ds_v3[(lh, vt)] = p.dma("sync", v3scr[vt * 128:(vt + 1) * 128, c0:c0 + 2048],
                                        vbf[vt % 2], inc=f"dst{vt % 2}")
                p.wait("tensor", "ve", ve_ev[vt])
                for c in range(4):
                    p.op("tensor", lambda t, vt=vt, c=c: t.matmul(
                        PM2[64:72, c * 512:(c + 1) * 512], rwbf[:, vt * 8:(vt + 1) * 8],
                        vbf[vt % 2][:, c * 512:(c + 1) * 512],
                        start=(vt == 0), stop=(vt == KT - 1)), inc="pe")
            pe_pj = p.c["pe"]
            p.wait("vector", "pe", pe_pj)
            p.op("vector", lambda v, lh=lh: v.tensor_copy(projv[lh][:],
                                                          PM2[64:72, 0:2048]), inc="ve")

        pe_st3 = [0] * KT
        p.wait("tensor", "ve", p.c["ve"])
        p.wait("gpsimd", "pe", p.c["pe"])
        ve_lpart_half = [0, 0]
        for lh in range(2):
            c0 = lh * 2048
            for kt in range(KT):
                p.wait("gpsimd", "dst0", ds_v3[(1, KT - 2)])
                p.wait("gpsimd", "dst1", ds_v3[(1, KT - 1)])
                if kt >= 2 or lh == 1:
                    p.wait("gpsimd", "pe", pe_st3[kt - 2])
                dgv = p.dma("gpsimd", vbf[kt % 2], v3scr[kt * 128:(kt + 1) * 128,
                                                            c0:c0 + 2048], inc=f"dre{kt % 2}")
                p.wait("scalar", f"dre{kt % 2}", dgv)
                if kt >= 1 or lh == 1:
                    p.wait("scalar", "pe", pe_st3[kt - 1])
                se = p.op("scalar", lambda s, kt=kt: s.activation(
                    v2rd[0], vbf[kt % 2], AF.Square), inc="se")
                p.wait("tensor", "se", se)
                if kt == 0 and lh == 1:
                    p.wait("tensor", "ve", ve_lpart_half[0])
                for c in range(4):
                    p.op("tensor", lambda t, kt=kt, c=c: t.matmul(
                        PM[0:1, c * 512:(c + 1) * 512], onesb[:],
                        vbf[kt % 2][:, c * 512:(c + 1) * 512],
                        start=(kt == 0), stop=(kt == KT - 1)), inc="pe")
                    pe_st3[kt] = p.op("tensor", lambda t, kt=kt, c=c: t.matmul(
                        PM2[0:1, c * 512:(c + 1) * 512], onesb[:],
                        v2rd[0][:, c * 512:(c + 1) * 512],
                        start=(kt == 0), stop=(kt == KT - 1)), inc="pe")
            p.wait("vector", "pe", p.c["pe"])
            finalize_512(PM[0:1, 0:2048], PM2[0:1, 0:2048], 2048, nf=D)
            p.wait("sync", "se", p.c["se"])
            p.dma("sync", mu3o[0:1, c0:c0 + 2048], rowA[0:1, 0:2048], inc="dmu")
            p.dma("sync", inv3o[0:1, c0:c0 + 2048], rowB[0:1, 0:2048], inc="dmu")
            p.seq("vector", lambda v: v.tensor_tensor(PM[0:1, 0:2048], rowA[0:1, 0:2048],
                                                      rowB[0:1, 0:2048], ALU.mult))
            p.wait("sync", "ve", p.c["ve"])
            p.wait("sync", "dmu", p.c["dmu"])
            dsq = p.dma("sync", rowA[0:1, 0:2048], qmn_d[0:1, c0:c0 + 2048], inc="dmu")
            p.wait("vector", "dmu", dsq)
            p.seq("vector", lambda v: v.tensor_tensor(PM[0:1, 0:2048], PM[0:1, 0:2048],
                                                      rowA[0:1, 0:2048], ALU.mult))
            p.seq("vector", lambda v, lh=lh: v.tensor_reduce(smr[0:1, lh:lh + 1],
                                                             PM[0:1, 0:2048], AX.X, ALU.add))
            p.seq("vector", lambda v: v.tensor_tensor(rowB[0:1, 0:2048], rowB[0:1, 0:2048],
                                                      rowA[0:1, 0:2048], ALU.mult))
            ve_wqi = p.c["ve"]
            p.wait("tensor", "ve", ve_wqi)
            for c in range(4):
                pe = p.op("tensor", lambda t, c=c: t.matmul(
                    PM2[64:72, c * 512:(c + 1) * 512], ones1[0:1, 0:8],
                    rowB[0:1, c * 512:(c + 1) * 512], start=True, stop=True), inc="pe")
            p.wait("vector", "pe", p.c["pe"])
            p.op("vector", lambda v, lh=lh: v.tensor_tensor(t8s[:], projv[lh][:],
                                                            PM2[64:72, 0:2048], ALU.mult),
                 inc="ve")
            ve_lpart_half[lh] = p.seq("vector", lambda v, lh=lh: v.tensor_reduce(
                sm72[64:72, 1 + lh:2 + lh], t8s[:], AX.X, ALU.add))
        p.seq("vector", lambda v: v.tensor_tensor(sm72[64:72, 3:4], sm72[64:72, 1:2],
                                                  sm72[64:72, 2:3], ALU.add))
        ve_s2t = p.seq("vector", lambda v: v.tensor_tensor(smr[0:1, 2:3], smr[0:1, 0:1],
                                                           smr[0:1, 1:2], ALU.add))
        p.wait("tensor", "ve", ve_s2t)
        pe = p.op("tensor", lambda t: t.matmul(PM2[64:72, 0:1], ones1[0:1, 0:8],
                                               smr[0:1, 2:3], start=True, stop=True), inc="pe")
        p.wait("vector", "pe", pe)
        p.op("vector", lambda v: v.tensor_tensor(sm72[64:72, 4:5], sm72[64:72, 0:1],
                                                 PM2[64:72, 0:1], ALU.mult), inc="ve")
        ve_lp = p.seq("vector", lambda v: v.tensor_tensor(sm72[64:72, 5:6], sm72[64:72, 3:4],
                                                          sm72[64:72, 4:5], ALU.subtract))
        p.wait("sync", "ve", ve_lp)
        p.dma("sync", lparto, sm72[64:72, 5:6], inc="dlp")
        p.dma("sync", v3o, v3scr, inc="dlp")
        p.dma_done_waits("sync")

        with blk_cm as blk:
            p.finalize(blk)
    return nc



def _cc(a):
    return np.ascontiguousarray(a)


def make_consts():
    sel4 = np.zeros((128, 4), np.float32)
    for k in range(128):
        sel4[k, k // 32] = 1.0
    sel4t = _cc(sel4.T).astype(BF)
    r32f = np.zeros((128, D), np.float32)
    for n in range(D):
        r32f[n // 32, n] = 1.0
    return sel4, sel4t, r32f.astype(BF), np.eye(4, dtype=np.float32)


def sel16_mat(n_e, width):
    s = np.zeros((n_e, width), np.float32)
    for p_ in range(width):
        e = p_ // (width // n_e) if False else p_ // 16
        if e < n_e:
            s[e, p_] = 1.0
    return s


def run_launch_a(inp):
    ncA = build_ffn_launch("A")
    sel4, sel4t, r32f_bf, ident4 = make_consts()
    x = inp["L0_attn_data"]
    xf = x.reshape(4 * 4096, D)
    wT0 = _cc(inp["L0_attn_weight"].T)
    maskT = _cc(inp["L0_attn_mask"].T)
    in_maps = []
    for c in range(8):
        in_maps.append({
            "xT": _cc(xf[c * M:(c + 1) * M, :].T),
            "wT": wT0, "maskT": maskT,
            "sel4": sel4, "sel4t": sel4t, "r32f": r32f_bf, "ident": ident4,
        })
    res = run_bass_kernel_spmd(ncA, in_maps, core_ids=list(range(8)))
    return [r["vout"] for r in res.results]


def run_launch_b(inp, v1):
    ncB = build_ffn_launch("B")
    wTb = _cc(inp["L0_ffn_weight"].T)
    A3 = inp["L0_ffn_lora_A"]
    B3 = inp["L0_ffn_lora_B"]
    acatT = _cc(A3.transpose(2, 0, 1).reshape(D, 48))
    bcatL = _cc(B3.transpose(0, 2, 1).reshape(48, D))
    biasb = _cc(inp["L0_ffn_bias"].reshape(32, 128).T)
    lsi = inp["L0_ffn_lora_start_indices"].reshape(1, 4).astype(np.int32)
    ero3 = np.arange(3, dtype=np.float32).reshape(1, 3)
    sel16 = sel16_mat(3, 48)
    in_maps = []
    for c in range(8):
        b = c // 2
        ccst = np.zeros((1, 8), np.float32)
        ccst[0, 0] = b + 0.5
        in_maps.append({
            "vin": v1[c], "wT": wTb, "biasb": biasb,
            "acatT": acatT, "bcatL": bcatL, "lsi": lsi, "ccst": ccst,
            "ero3": ero3, "sel16": sel16,
        })
    res = run_bass_kernel_spmd(ncB, in_maps, core_ids=list(range(8)))
    return [r["vout"] for r in res.results]


def run_launch_c(inp, v2):
    ncC = build_attn_launch()
    g = lambda k: inp["L1_attn_" + k].astype(np.float32, copy=False)
    winT = _cc(g("Win").T)
    w1T = _cc(g("W1").transpose(0, 2, 1))
    w2T = _cc(g("W2").transpose(0, 2, 1))
    woutT = _cc(g("Wout").T)
    rwT = _cc(inp["L1_ffn_router_W"].astype(np.float32).T)
    vecs = [g("ln1_w"), g("ln1_b"), g("phz_re"), g("phz_im"), g("lc_re"), g("lc_im"),
            g("sc_lin"), g("ln2_w"), g("ln2_b"), g("b2")]
    svec = np.zeros((128, 160), np.float32)
    for i in range(4):
        for v in range(10):
            for gg in range(4):
                svec[:, (i * 10 + v) * 4 + gg] = vecs[v][i][gg * 128:(gg + 1) * 128]
    b1b = np.zeros((128, 64), np.float32)
    for i in range(4):
        for ft in range(16):
            b1b[:, i * 16 + ft] = g("b1")[i][ft * 128:(ft + 1) * 128]
    binb = _cc(g("bin").reshape(4, 128).T)
    lnlb = np.zeros((128, 8), np.float32)
    for gg in range(4):
        lnlb[:, 2 * gg] = g("lnl_w")[gg * 128:(gg + 1) * 128]
        lnlb[:, 2 * gg + 1] = g("lnl_b")[gg * 128:(gg + 1) * 128]
    boutb = _cc(g("bout").reshape(32, 128).T)
    qmn = np.zeros((1, L), np.float32)
    qmn[0, 611:4095] = 1.0 / 3484.0
    in_maps = []
    for c in range(8):
        b = c // 2
        v2full = np.concatenate([np.asarray(v2[2 * b]), np.asarray(v2[2 * b + 1])], axis=1)
        in_maps.append({
            "v2f": _cc(v2full), "winT": winT, "w1T": w1T, "w2T": w2T, "woutT": woutT,
            "svec": svec, "b1b": b1b, "binb": binb, "lnlb": lnlb, "boutb": boutb,
            "rwT": rwT, "qmn": qmn,
        })
    res = run_bass_kernel_spmd(ncC, in_maps, core_ids=list(range(8)))
    return res.results


def run_launch_d(inp, rc):
    ncD = build_ffn_launch("D")
    wTd = _cc(inp["L1_ffn_W"].T)
    A8 = inp["L1_ffn_A"]
    B8 = inp["L1_ffn_B"]
    acatT = _cc(A8.transpose(2, 0, 1).reshape(D, 128))
    bcatL = _cc(B8.transpose(0, 2, 1).reshape(128, D))
    biasb = _cc(inp["L1_ffn_b"].reshape(32, 128).T)
    rb = inp["L1_ffn_router_b"].astype(np.float32).reshape(1, 8)
    sel16t = sel16_mat(8, 128)
    in_maps = []
    for c in range(8):
        hh = c % 2
        t0 = hh * M
        in_maps.append({
            "vin": _cc(np.asarray(rc[c]["v3o"])[:, t0:t0 + M]),
            "wT": wTd, "biasb": biasb, "acatT": acatT, "bcatL": bcatL,
            "lpart": np.asarray(rc[c]["lparto"]).reshape(1, 8),
            "rb": rb, "sel16t": sel16t,
            "muin": _cc(np.asarray(rc[c]["mu3o"])[:, t0:t0 + M]),
            "invin": _cc(np.asarray(rc[c]["inv3o"])[:, t0:t0 + M]),
        })
    res = run_bass_kernel_spmd(ncD, in_maps, core_ids=list(range(8)))
    return [r["xout"] for r in res.results]


def kernel(**inp):
    v1 = run_launch_a(inp)
    v2 = run_launch_b(inp, v1)
    rc = run_launch_c(inp, v2)
    x4 = run_launch_d(inp, rc)
    out = np.empty((4, 4096, 4096), np.float32)
    for c in range(8):
        b, hh = c // 2, c % 2
        out[b, hh * M:(hh + 1) * M, :] = np.asarray(x4[c]).T
    return out
